# version 02
# baseline (speedup 1.0000x reference)
import numpy as np
import ml_dtypes
import concourse.bass as bass
import concourse.mybir as mybir
from concourse.bass_utils import run_bass_kernel_spmd

F32 = mybir.dt.float32
BF16 = mybir.dt.bfloat16
U32 = mybir.dt.uint32
AX = mybir.AxisListType.X
OP = mybir.AluOpType
AF = mybir.ActivationFunctionType

D = 4096
I = 11008
NC_ROWS = 1024
NCH = NC_ROWS // 128
ICH = I // 128
DCH = D // 128
BLK = 8
N_CORES = 8
EPS = 1e-5


def build_nc():
    nc = bass.Bass("TRN2", target_bir_lowering=False)
    x_h = nc.dram_tensor("x", [NC_ROWS, D], F32, kind="ExternalInput").ap()
    wg_h = nc.dram_tensor("wg", [ICH, 128, D], F32, kind="ExternalInput").ap()
    wu_h = nc.dram_tensor("wu", [ICH, 128, D], F32, kind="ExternalInput").ap()
    wd_h = nc.dram_tensor("wd", [ICH, 128, D], F32, kind="ExternalInput").ap()
    ag_h = nc.dram_tensor("ag", [128, ICH], F32, kind="ExternalInput").ap()
    au_h = nc.dram_tensor("au", [128, ICH], F32, kind="ExternalInput").ap()
    adb_h = nc.dram_tensor("adb", [128, D], F32, kind="ExternalInput").ap()
    id_h = nc.dram_tensor("ident", [128, 128], BF16, kind="ExternalInput").ap()
    y_h = nc.dram_tensor("y", [NC_ROWS, D], F32, kind="ExternalOutput").ap()
    s_d = nc.dram_tensor("s_d", [NCH, 128, D], BF16, kind="Internal").ap()
    h_d = nc.dram_tensor("h_d", [ICH, 128, NC_ROWS], BF16, kind="Internal").ap()

    blocks = []
    ic0 = 0
    while ic0 < ICH:
        blocks.append(list(range(ic0, min(ic0 + BLK, ICH))))
        ic0 += BLK
    NBLK = len(blocks)
    NHALF_S = NCH // 2

    from contextlib import ExitStack
    es = ExitStack()
    with es:
        ag_sb = es.enter_context(nc.sbuf_tensor("ag_sb", [128, ICH], F32))
        au_sb = es.enter_context(nc.sbuf_tensor("au_sb", [128, ICH], F32))
        id_sb = es.enter_context(nc.sbuf_tensor("id_sb", [128, 128], BF16))
        es2 = es.enter_context(ExitStack())
        xnT = es2.enter_context(nc.sbuf_tensor("xnT", [128, DCH * NC_ROWS], BF16))

        eng_names = ["s_c", "s_v_xm", "s_s_sq", "s_v_var", "s_s_sd", "s_v_xn",
                     "s_t_tr", "s_v_tp", "s_t_g", "s_s_sil", "s_v_h",
                     "s_t_d", "s_v_d", "s_v_e1", "s_s_e", "s_v_e2"]
        E = {}
        for nm in eng_names:
            E[nm] = es.enter_context(nc.semaphore(nm))
        dma_names = ["s_x", "s_sd", "s_w", "s_hd", "s_hr", "s_w2",
                     "s_xe", "s_se", "s_y"]
        DS = {}
        for nm in dma_names:
            DS[nm] = (es.enter_context(nc.semaphore(nm + "0")),
                      es.enter_context(nc.semaphore(nm + "1")))
        s_c = E["s_c"]
        block = es.enter_context(nc.Block())

        @block.sync
        def _(sync):
            sync.dma_start(out=ag_sb[:], in_=ag_h).then_inc(s_c, 16)
            sync.dma_start(out=au_sb[:], in_=au_h).then_inc(s_c, 16)
            sync.dma_start(out=id_sb[:], in_=id_h).then_inc(s_c, 16)

        with (
            nc.sbuf_tensor("x_sb", [128, 2 * D], F32) as x_sb,
            nc.sbuf_tensor("xm_sb", [128, 2 * D], F32) as xm_sb,
            nc.sbuf_tensor("sq_sb", [128, 2 * D], F32) as sq_sb,
            nc.sbuf_tensor("xn_bf", [128, 2 * D], BF16) as xn_bf,
            nc.sbuf_tensor("st0", [128, 8], F32) as st0,
            nc.psum_tensor("ps_tr", [128, 4096], BF16) as ps_tr,
        ):
            def xs(c):
                return x_sb[:, (c % 2) * D:(c % 2) * D + D]

            def xms(c):
                return xm_sb[:, (c % 2) * D:(c % 2) * D + D]

            def sqs(c):
                return sq_sb[:, (c % 2) * D:(c % 2) * D + D]

            def xns(c):
                return xn_bf[:, (c % 2) * D:(c % 2) * D + D]

            @block.sync
            def _(sync):
                for c in range(NCH):
                    if c >= 2:
                        sync.wait_ge(E["s_v_xm"], c - 1)
                    sync.dma_start(out=xs(c), in_=x_h[c * 128:(c + 1) * 128, :]
                                   ).then_inc(DS["s_x"][c % 2], 16)

            @block.vector
            def _(v):
                for c in range(NCH):
                    v.wait_ge(DS["s_x"][c % 2], 16 * (c // 2 + 1))
                    if c >= 2:
                        v.wait_ge(E["s_s_sq"], c - 1)
                    v.reduce_sum(st0[:, 0:1], xs(c), axis=AX)
                    v.drain()
                    v.tensor_scalar(out=st0[:, 1:2], in0=st0[:, 0:1], scalar1=1.0 / D,
                                    scalar2=None, op0=OP.mult)
                    v.drain()
                    v.tensor_scalar(out=xms(c), in0=xs(c), scalar1=st0[:, 1:2],
                                    scalar2=None, op0=OP.subtract).then_inc(E["s_v_xm"], 1)
                    v.wait_ge(E["s_s_sq"], c + 1)
                    v.reduce_sum(st0[:, 2:3], sqs(c), axis=AX)
                    v.drain()
                    v.tensor_scalar(out=st0[:, 3:4], in0=st0[:, 2:3], scalar1=1.0 / D,
                                    scalar2=EPS, op0=OP.mult, op1=OP.add).then_inc(E["s_v_var"], 1)
                    v.wait_ge(E["s_s_sd"], c + 1)
                    v.reciprocal(st0[:, 5:6], st0[:, 4:5])
                    v.drain()
                    if c >= 2:
                        v.wait_ge(DS["s_sd"][c % 2], 16 * (c // 2))
                        v.wait_ge(E["s_t_tr"], 4 * (c - 1))
                    v.tensor_scalar(out=xns(c), in0=xms(c), scalar1=st0[:, 5:6],
                                    scalar2=None, op0=OP.mult).then_inc(E["s_v_xn"], 1)
                    for t in range(4):
                        v.wait_ge(E["s_t_tr"], 4 * c + t + 1)
                        v.tensor_copy(
                            xnT.rearrange("p (j tt) -> p j tt", j=DCH)
                            [:, 8 * t:8 * t + 8, c * 128:(c + 1) * 128],
                            ps_tr[:, t * 1024:(t + 1) * 1024]
                            .rearrange("p (j t128) -> p j t128", j=8),
                        ).then_inc(E["s_v_tp"], 1)

            @block.scalar
            def _(s):
                for c in range(NCH):
                    s.wait_ge(E["s_v_xm"], c + 1)
                    if c >= 2:
                        s.wait_ge(E["s_v_var"], c - 1)
                    s.activation(out=sqs(c), in_=xms(c), func=AF.Square).then_inc(E["s_s_sq"], 1)
                    s.wait_ge(E["s_v_var"], c + 1)
                    s.activation(out=st0[:, 4:5], in_=st0[:, 3:4],
                                 func=AF.Sqrt).then_inc(E["s_s_sd"], 1)

            @block.tensor
            def _(t):
                t.wait_ge(s_c, 48)
                for c in range(NCH):
                    t.wait_ge(E["s_v_xn"], c + 1)
                    for tt in range(4):
                        if c >= 1:
                            t.wait_ge(E["s_v_tp"], 4 * (c - 1) + tt + 1)
                        for jj in range(8):
                            j = 8 * tt + jj
                            ti = t.transpose(
                                ps_tr[:, tt * 1024 + jj * 128:tt * 1024 + (jj + 1) * 128],
                                xns(c)[:, j * 128:(j + 1) * 128],
                                id_sb[:],
                            )
                            if jj == 7:
                                ti.then_inc(E["s_t_tr"], 1)

            @block.gpsimd
            def _(g):
                for c in range(NCH):
                    g.wait_ge(E["s_v_xn"], c + 1)
                    g.dma_start(out=s_d[c], in_=xns(c)).then_inc(DS["s_sd"][c % 2], 16)

        with (
            nc.sbuf_tensor("wg_sb", [128, 2 * D], BF16) as wg_sb,
            nc.sbuf_tensor("wu_sb", [128, 2 * D], BF16) as wu_sb,
            nc.sbuf_tensor("sil_sb", [128, 2 * NC_ROWS], F32) as sil_sb,
            nc.sbuf_tensor("h_st", [128, 2 * NC_ROWS], BF16) as h_st,
            nc.psum_tensor("ps_g", [128, 2 * NC_ROWS], F32) as ps_g,
            nc.psum_tensor("ps_u", [128, 2 * NC_ROWS], F32) as ps_u,
        ):
            def wgs(ic):
                return wg_sb[:, (ic % 2) * D:(ic % 2) * D + D]

            def wus(ic):
                return wu_sb[:, (ic % 2) * D:(ic % 2) * D + D]

            def psg(ic):
                return ps_g[:, (ic % 2) * NC_ROWS:(ic % 2) * NC_ROWS + NC_ROWS]

            def psu(ic):
                return ps_u[:, (ic % 2) * NC_ROWS:(ic % 2) * NC_ROWS + NC_ROWS]

            def sils(ic):
                return sil_sb[:, (ic % 2) * NC_ROWS:(ic % 2) * NC_ROWS + NC_ROWS]

            def hsts(ic):
                return h_st[:, (ic % 2) * NC_ROWS:(ic % 2) * NC_ROWS + NC_ROWS]

            @block.gpsimd
            def _(g):
                g.wait_ge(E["s_t_tr"], 4 * NCH)
                for ic in range(ICH):
                    if ic >= 2:
                        g.wait_ge(E["s_t_g"], 2 * (ic - 1))
                    g.dma_start(out=wgs(ic), in_=wg_h[ic],
                                max_dma_last_dim=2048).then_inc(DS["s_w"][ic % 2], 16)
                    g.dma_start(out=wus(ic), in_=wu_h[ic],
                                max_dma_last_dim=2048).then_inc(DS["s_w"][ic % 2], 16)

            @block.tensor
            def _(t):
                t.wait_ge(E["s_v_tp"], 4 * NCH)
                for ic in range(ICH):
                    t.wait_ge(DS["s_w"][ic % 2], 32 * (ic // 2 + 1))
                    if ic >= 2:
                        t.wait_ge(E["s_s_sil"], ic - 1)
                        t.wait_ge(E["s_v_h"], ic - 1)
                    for j in range(DCH):
                        for th in range(2):
                            mi = t.matmul(psg(ic)[:, th * 512:(th + 1) * 512],
                                          wgs(ic)[:, j * 128:(j + 1) * 128],
                                          xnT[:, bass.ds(j * NC_ROWS + th * 512, 512)],
                                          start=(j == 0), stop=(j == DCH - 1))
                            if j == DCH - 1 and th == 1:
                                mi.then_inc(E["s_t_g"], 1)
                    for j in range(DCH):
                        for th in range(2):
                            mi = t.matmul(psu(ic)[:, th * 512:(th + 1) * 512],
                                          wus(ic)[:, j * 128:(j + 1) * 128],
                                          xnT[:, bass.ds(j * NC_ROWS + th * 512, 512)],
                                          start=(j == 0), stop=(j == DCH - 1))
                            if j == DCH - 1 and th == 1:
                                mi.then_inc(E["s_t_g"], 1)

            @block.scalar
            def _(s):
                for ic in range(ICH):
                    s.wait_ge(E["s_t_g"], 2 * ic + 1)
                    if ic >= 2:
                        s.wait_ge(E["s_v_h"], ic - 1)
                    s.activation(out=sils(ic), in_=psg(ic), func=AF.Silu,
                                 scale=ag_sb[:, ic:ic + 1]).then_inc(E["s_s_sil"], 1)

            @block.vector
            def _(v):
                for ic in range(ICH):
                    v.wait_ge(E["s_t_g"], 2 * ic + 2)
                    v.wait_ge(E["s_s_sil"], ic + 1)
                    if ic >= 2:
                        v.wait_ge(DS["s_hd"][ic % 2], 16 * (ic // 2))
                    v.scalar_tensor_tensor(out=hsts(ic), in0=psu(ic),
                                           scalar=au_sb[:, ic:ic + 1], in1=sils(ic),
                                           op0=OP.mult, op1=OP.mult).then_inc(E["s_v_h"], 1)

            @block.sync
            def _(sync):
                for ic in range(ICH):
                    sync.wait_ge(E["s_v_h"], ic + 1)
                    sync.dma_start(out=h_d[ic], in_=hsts(ic)).then_inc(DS["s_hd"][ic % 2], 16)

        es2.close()
        with (
            nc.sbuf_tensor("h_sb", [128, 2 * BLK * NC_ROWS], BF16) as h_sb,
            nc.sbuf_tensor("wd_sb", [128, 2 * BLK * 2048], BF16) as wd_sb,
            nc.sbuf_tensor("oac", [128, NCH * 2048], F32) as oac,
            nc.sbuf_tensor("adb_sb", [128, D], F32) as adb_sb,
            nc.sbuf_tensor("x_ep", [128, 2 * 1024], F32) as x_ep,
            nc.sbuf_tensor("s_ep", [128, 2 * 1024], BF16) as s_ep,
            nc.sbuf_tensor("o1_ep", [128, 2 * 1024], F32) as o1_ep,
            nc.sbuf_tensor("g_ep", [128, 2 * 1024], F32) as g_ep,
            nc.psum_tensor("ps_d", [128, 2 * 2048], F32) as ps_d,
        ):
            def hbs(b):
                return h_sb[:, (b % 2) * BLK * NC_ROWS:]

            def wds(b):
                return wd_sb[:, (b % 2) * BLK * 2048:]

            def psd(ts):
                return ps_d[:, (ts % 2) * 2048:(ts % 2) * 2048 + 2048]

            def gidx(half, b):
                return half * NBLK + b

            cum_par = {0: 0, 1: 0}
            blk_cum = {}
            for half in range(2):
                for b, blk in enumerate(blocks):
                    gi = gidx(half, b)
                    cum_par[gi % 2] += len(blk)
                    blk_cum[gi] = cum_par[gi % 2]

            ICH_PAR = (ICH + 1) // 2

            @block.sync
            def _(sync):
                sync.wait_ge(E["s_t_g"], 2 * ICH)
                sync.wait_ge(DS["s_hd"][0], 16 * ICH_PAR)
                sync.wait_ge(DS["s_hd"][1], 16 * (ICH - ICH_PAR))
                sync.wait_ge(DS["s_sd"][0], 16 * NHALF_S)
                sync.wait_ge(DS["s_sd"][1], 16 * NHALF_S)
                for half in range(2):
                    for b, blk in enumerate(blocks):
                        gi = gidx(half, b)
                        if gi >= 2:
                            sync.wait_ge(E["s_t_d"], 8 * (gi - 1))
                        for l, ic in enumerate(blk):
                            sync.dma_start(
                                out=hbs(gi)[:, l * NC_ROWS:(l + 1) * NC_ROWS],
                                in_=h_d[ic]).then_inc(DS["s_hr"][gi % 2], 16)
                    for ts in range(NCH):
                        for q in range(2):
                            ei = (half * NCH + ts) * 2 + q
                            col = half * 2048 + q * 1024
                            sl = (ei % 2) * 1024
                            if ei >= 2:
                                sync.wait_ge(E["s_v_e2"], ei - 1)
                            sync.dma_start(
                                out=x_ep[:, sl:sl + 1024],
                                in_=x_h[ts * 128:(ts + 1) * 128, col:col + 1024],
                            ).then_inc(DS["s_xe"][ei % 2], 16)
                            sync.dma_start(
                                out=s_ep[:, sl:sl + 1024],
                                in_=s_d[ts][:, col:col + 1024],
                            ).then_inc(DS["s_se"][ei % 2], 16)
                            sync.wait_ge(E["s_v_e2"], ei + 1)
                            sync.dma_start(
                                out=y_h[ts * 128:(ts + 1) * 128, col:col + 1024],
                                in_=o1_ep[:, sl:sl + 1024],
                            ).then_inc(DS["s_y"][ei % 2], 16)
                sync.wait_ge(DS["s_y"][0], 16 * 16)
                sync.wait_ge(DS["s_y"][1], 16 * 16)

            @block.gpsimd
            def _(g):
                g.wait_ge(E["s_t_g"], 2 * ICH)
                g.wait_ge(DS["s_hd"][0], 16 * ICH_PAR)
                g.wait_ge(DS["s_hd"][1], 16 * (ICH - ICH_PAR))
                g.dma_start(out=adb_sb[:], in_=adb_h).then_inc(s_c, 16)
                for half in range(2):
                    for b, blk in enumerate(blocks):
                        gi = gidx(half, b)
                        if gi >= 2:
                            g.wait_ge(E["s_t_d"], 8 * (gi - 1))
                        for l, ic in enumerate(blk):
                            g.dma_start(
                                out=wds(gi)[:, l * 2048:(l + 1) * 2048],
                                in_=wd_h[ic][:, half * 2048:(half + 1) * 2048],
                                max_dma_last_dim=2048).then_inc(DS["s_w2"][gi % 2], 16)

            @block.tensor
            def _(t):
                t.wait_ge(E["s_v_h"], ICH)
                t.wait_ge(E["s_s_sil"], ICH)
                for half in range(2):
                    for b, blk in enumerate(blocks):
                        gi = gidx(half, b)
                        L = len(blk)
                        t.wait_ge(DS["s_hr"][gi % 2], 16 * blk_cum[gi])
                        t.wait_ge(DS["s_w2"][gi % 2], 16 * blk_cum[gi])
                        for ts in range(NCH):
                            grp = 8 * gi + ts
                            if grp >= 2:
                                t.wait_ge(E["s_v_d"], grp - 1)
                            for l in range(L):
                                for dmt in range(4):
                                    mi = t.matmul(
                                        psd(ts)[:, dmt * 512:(dmt + 1) * 512],
                                        hbs(gi)[:, l * NC_ROWS + ts * 128:l * NC_ROWS + ts * 128 + 128],
                                        wds(gi)[:, l * 2048 + dmt * 512:l * 2048 + (dmt + 1) * 512],
                                        start=(l == 0), stop=(l == L - 1))
                                    if l == L - 1 and dmt == 3:
                                        mi.then_inc(E["s_t_d"], 1)

            @block.vector
            def _(v):
                for half in range(2):
                    v.drain()
                    for b in range(NBLK):
                        gi = gidx(half, b)
                        for ts in range(NCH):
                            grp = 8 * gi + ts
                            v.wait_ge(E["s_t_d"], grp + 1)
                            dst = oac[:, ts * 2048:(ts + 1) * 2048]
                            if b == 0:
                                v.tensor_copy(dst, psd(ts)).then_inc(E["s_v_d"], 1)
                            else:
                                v.tensor_add(dst, dst, psd(ts)).then_inc(E["s_v_d"], 1)
                    v.drain()
                    for ts in range(NCH):
                        for q in range(2):
                            ei = (half * NCH + ts) * 2 + q
                            sl = (ei % 2) * 1024
                            v.wait_ge(DS["s_xe"][ei % 2], 16 * (ei // 2 + 1))
                            v.wait_ge(s_c, 64)
                            if ei >= 2:
                                v.wait_ge(DS["s_y"][ei % 2], 16 * (ei // 2))
                            v.tensor_mul(o1_ep[:, sl:sl + 1024],
                                         oac[:, ts * 2048 + q * 1024:ts * 2048 + q * 1024 + 1024],
                                         adb_sb[:, half * 2048 + q * 1024:half * 2048 + q * 1024 + 1024]
                                         ).then_inc(E["s_v_e1"], 1)
                            v.wait_ge(E["s_s_e"], ei + 1)
                            v.wait_ge(DS["s_se"][ei % 2], 16 * (ei // 2 + 1))
                            ge = g_ep[:, sl:sl + 1024]
                            se = s_ep[:, sl:sl + 1024]
                            xe = x_ep[:, sl:sl + 1024]
                            v.tensor_mul(ge, ge, se)
                            v.drain()
                            v.tensor_add(o1_ep[:, sl:sl + 1024], ge, xe).then_inc(E["s_v_e2"], 1)

            @block.scalar
            def _(s):
                for half in range(2):
                    for ts in range(NCH):
                        for q in range(2):
                            ei = (half * NCH + ts) * 2 + q
                            sl = (ei % 2) * 1024
                            s.wait_ge(E["s_v_e1"], ei + 1)
                            s.activation(out=g_ep[:, sl:sl + 1024],
                                         in_=o1_ep[:, sl:sl + 1024],
                                         func=AF.Silu).then_inc(E["s_s_e"], 1)

    return nc


_NC_CACHE = None


def kernel(L0_attn_x, L0_ffn_Wg, L0_ffn_Wu, L0_ffn_Wd, L0_ffn_ag, L0_ffn_au, L0_ffn_ad):
    global _NC_CACHE
    x = np.ascontiguousarray(np.asarray(L0_attn_x, dtype=np.float32)).reshape(8192, D)
    Wg = np.asarray(L0_ffn_Wg, dtype=np.float32)
    Wu = np.asarray(L0_ffn_Wu, dtype=np.float32)
    Wd = np.asarray(L0_ffn_Wd, dtype=np.float32)
    ag = np.asarray(L0_ffn_ag, dtype=np.float32)
    au = np.asarray(L0_ffn_au, dtype=np.float32)
    ad = np.asarray(L0_ffn_ad, dtype=np.float32)

    wg_dev = np.ascontiguousarray(
        Wg.reshape(ICH, 128, DCH, 128).transpose(0, 3, 2, 1).reshape(ICH, 128, D))
    wu_dev = np.ascontiguousarray(
        Wu.reshape(ICH, 128, DCH, 128).transpose(0, 3, 2, 1).reshape(ICH, 128, D))
    wd_dev = np.ascontiguousarray(Wd.T).reshape(ICH, 128, D)
    ag_dev = np.ascontiguousarray(ag.reshape(ICH, 128).T)
    au_dev = np.ascontiguousarray(au.reshape(ICH, 128).T)
    adb = np.ascontiguousarray(np.broadcast_to(ad[None, :], (128, D)))
    ident = np.eye(128, dtype=ml_dtypes.bfloat16)

    if _NC_CACHE is None:
        _NC_CACHE = build_nc()
    nc = _NC_CACHE

    in_maps = []
    for ci in range(N_CORES):
        in_maps.append(dict(
            x=x[ci * NC_ROWS:(ci + 1) * NC_ROWS],
            wg=wg_dev, wu=wu_dev, wd=wd_dev,
            ag=ag_dev, au=au_dev, adb=adb, ident=ident,
        ))
    res = run_bass_kernel_spmd(nc, in_maps, core_ids=list(range(N_CORES)))
    y = np.concatenate([r["y"] for r in res.results], axis=0)
    return y.reshape(4, 2048, D).astype(np.float32)
